# version 01 (final)
# baseline (speedup 1.0000x reference)
import numpy as np
import concourse.bass as bass
import concourse.mybir as mybir
from concourse.bass_utils import run_bass_kernel_spmd

F32 = mybir.dt.float32
BF16 = mybir.dt.bfloat16
I32 = mybir.dt.int32
U8 = mybir.dt.uint8
AF = mybir.ActivationFunctionType
ALU = mybir.AluOpType
AX = mybir.AxisListType

P = 128
D = 1024
DK = 8
H = 16
HD = 64
CH = 128
NCH = 16
SEQS = 2
TS = 512
NSUP = NCH * CH // TS
FF = 4096
FM = FF // P
LN_EPS = 1e-5
ATT_EPS = 1e-6
NL = 4

ENGINES = ("pe", "dve", "act", "pool", "sp")


class Sched:

    COMPUTE = {"pe", "dve", "act", "pool"}
    OWNER = {"pe": "pe", "dve": "dve", "act": "act", "pool": "pool"}

    def __init__(self):
        self.recs = []
        self.sem_h = {}
        self.n_inc = {}
        self.q = {e: [] for e in ENGINES}

    def add_sem(self, name, handle):
        self.sem_h[name] = handle
        self.n_inc[name] = 0

    def emit(self, eng, fn, inc=None, waits=(), inc_by=1, force_waits=()):
        assert eng in ENGINES
        wl = []
        for t in list(force_waits):
            if t is None:
                continue
            sm, v = t
            if v <= 0:
                continue
            assert v <= self.n_inc[sm], f"wait on future {sm}={v} > {self.n_inc[sm]}"
            wl.append((sm, v))
        for t in waits:
            if t is None:
                continue
            sm, v = t
            if v <= 0:
                continue
            assert v <= self.n_inc[sm], f"wait on future {sm}={v} > {self.n_inc[sm]}"
            if sm in self.COMPUTE and self.OWNER.get(sm) == eng:
                continue
            wl.append((sm, v))
        tok = None
        if inc is not None:
            self.n_inc[inc] += inc_by
            tok = (inc, self.n_inc[inc])
        self.recs.append({"eng": eng, "fn": fn, "sem": inc, "by": inc_by, "waits": wl})
        return tok

    def now(self, *sems):
        return [(sm, self.n_inc[sm]) for sm in sems]

    def bar(self, engines=("pe", "dve", "act")):
        cur = {e: self.n_inc[e] for e in engines}
        for e in engines:
            self.emit(e, lambda x: None,
                      waits=[(o, cur[o]) for o in engines if o != e])

    def finalize(self):
        used = {sm: set() for sm in self.n_inc}
        for r in self.recs:
            for (sm, v) in r["waits"]:
                used[sm].add(v)
        vmap = {}
        for sm in self.n_inc:
            if sm not in self.COMPUTE:
                continue
            keep = sorted(used[sm])
            m, rank = {}, 0
            ki = 0
            for old in range(1, self.n_inc[sm] + 1):
                if ki < len(keep) and keep[ki] == old:
                    rank += 1
                    ki += 1
                m[old] = rank
            vmap[sm] = m
        counters = {sm: 0 for sm in self.n_inc}
        waited = {e: {} for e in ENGINES}
        n_kept = 0
        for r in self.recs:
            eng, fn, sm, by = r["eng"], r["fn"], r["sem"], r["by"]
            wl = []
            for (wsm, wv) in r["waits"]:
                nv = vmap[wsm].get(wv, wv) if wsm in vmap else wv
                if waited[eng].get(wsm, 0) >= nv:
                    continue
                waited[eng][wsm] = nv
                wl.append((self.sem_h[wsm], nv))
            do_inc = False
            if sm is not None:
                if sm in self.COMPUTE:
                    cur = counters[sm] + by
                    counters[sm] = cur
                    if cur in used[sm]:
                        do_inc = True
                    else:
                        do_inc = False
                else:
                    do_inc = True
            if do_inc and sm in self.COMPUTE:
                newv = vmap[sm][counters[sm]]
                semh, b = self.sem_h[sm], by

                def run(e, fn=fn, wl=wl, semh=semh, b=b):
                    for (h, v) in wl:
                        e.wait_ge(h, v)
                    fn(e).then_inc(semh, b)
                n_kept += 1
            elif do_inc:
                semh, b = self.sem_h[sm], by

                def run(e, fn=fn, wl=wl, semh=semh, b=b):
                    for (h, v) in wl:
                        e.wait_ge(h, v)
                    fn(e).then_inc(semh, b)
                n_kept += 1
            else:
                def run(e, fn=fn, wl=wl):
                    for (h, v) in wl:
                        e.wait_ge(h, v)
                    r_ = fn(e)
                    _ = r_
                n_kept += 0
            self.q[eng].append(run)
        return n_kept


def build_program(n_layers=NL, do_tail=True, n_seqs=SEQS, stage=None):
    nc = bass.Bass("TRN2", target_bir_lowering=False,
                   detect_race_conditions=False)

    x_in = nc.dram_tensor("x", [SEQS, NCH * CH, D], F32, kind="ExternalInput").ap()
    out_d = nc.dram_tensor("out", [SEQS, NCH * CH, D], F32, kind="ExternalOutput").ap()
    wq_d = nc.dram_tensor("wq", [4, D, D], F32, kind="ExternalInput").ap()
    wk_d = nc.dram_tensor("wk", [4, D, D], F32, kind="ExternalInput").ap()
    wv_d = nc.dram_tensor("wv", [4, D, D], F32, kind="ExternalInput").ap()
    wo_d = nc.dram_tensor("wo", [4, D, D], F32, kind="ExternalInput").ap()
    w1_d = nc.dram_tensor("w1", [4, D, FF], F32, kind="ExternalInput").ap()
    w2_d = nc.dram_tensor("w2", [4, FF, D], F32, kind="ExternalInput").ap()
    bq_d = nc.dram_tensor("bq", [4, D], F32, kind="ExternalInput").ap()
    bk_d = nc.dram_tensor("bk", [4, D], F32, kind="ExternalInput").ap()
    bv_d = nc.dram_tensor("bv", [4, D], F32, kind="ExternalInput").ap()
    bo_d = nc.dram_tensor("bo", [4, D], F32, kind="ExternalInput").ap()
    b1_d = nc.dram_tensor("b1", [4, FF], F32, kind="ExternalInput").ap()
    b2_d = nc.dram_tensor("b2", [4, D], F32, kind="ExternalInput").ap()
    g1_d = nc.dram_tensor("g1", [4, D], F32, kind="ExternalInput").ap()
    be1_d = nc.dram_tensor("be1", [4, D], F32, kind="ExternalInput").ap()
    g2_d = nc.dram_tensor("g2", [4, D], F32, kind="ExternalInput").ap()
    be2_d = nc.dram_tensor("be2", [4, D], F32, kind="ExternalInput").ap()
    gf_d = nc.dram_tensor("gf", [D], F32, kind="ExternalInput").ap()
    bef_d = nc.dram_tensor("bef", [D], F32, kind="ExternalInput").ap()
    fw1_d = nc.dram_tensor("fw1", [D, FF], F32, kind="ExternalInput").ap()
    fb1_d = nc.dram_tensor("fb1", [FF], F32, kind="ExternalInput").ap()
    fw2_d = nc.dram_tensor("fw2", [FF, D], F32, kind="ExternalInput").ap()
    fb2_d = nc.dram_tensor("fb2", [D], F32, kind="ExternalInput").ap()
    w2s_d = [nc.dram_tensor(f"w2s{l}", [FF, D], BF16, kind="Internal").ap()
             for l in range(4)]
    fw2s_d = nc.dram_tensor("fw2s", [FF, D], BF16, kind="Internal").ap()

    from contextlib import ExitStack
    with ExitStack() as ctx:
        def sbuf(name, shape, dt=F32):
            return ctx.enter_context(nc.sbuf_tensor(name, shape, dt))

        xres = sbuf("xres", [P, NCH * D])
        wA = sbuf("wA", [P, 32768], BF16)
        work = sbuf("work", [P, 16448], BF16)
        xt = sbuf("xt", [P, 4096], BF16)
        w2t = sbuf("w2t", [P, 2048], BF16)
        ssb = sbuf("ssb", [P, 2048], BF16)
        kvst = sbuf("kvst", [P, 520])
        kvbf = sbuf("kvbf", [P, 520], BF16)
        stg = sbuf("stg", [P, 512])
        repl = sbuf("repl", [P, 1024])
        f32s = sbuf("f32s", [P, 1024])
        sqd = sbuf("sqd", [P, 1024])
        i32a = sbuf("i32a", [P, 512], I32)
        i32b = sbuf("i32b", [P, 512], I32)
        u8m = sbuf("u8m", [P, 512], U8)
        smallv = sbuf("smallv", [P, 192])
        rowb = sbuf("rowb", [1, 3200], BF16)
        rowf = sbuf("rowf", [1, 512])
        constp = sbuf("constp", [P, 2])
        identf = sbuf("identf", [P, 128])
        identb = sbuf("identb", [P, 128], BF16)
        maskb = sbuf("maskb", [P, 128], BF16)

        PS0 = ctx.enter_context(nc.psum_tensor("PS0", [P, 2048], F32))
        PS1 = ctx.enter_context(nc.psum_tensor("PS1", [P, 2048], F32))

        S = Sched()
        for s in ("pe", "dve", "act", "pool", "sp", "io", "w2", "st"):
            S.add_sem(s, ctx.enter_context(nc.semaphore(s)))

        blk = ctx.enter_context(nc.Block())

        def dma(eng, sem, out_ap, in_ap, waits=(), slow=False):
            chain = [(sem, S.n_inc[sem])]
            return S.emit(eng, lambda e: e.dma_start(
                out=out_ap, in_=in_ap, allow_slow_non_contiguous=slow),
                inc=sem, waits=list(waits) + chain, inc_by=16)

        def qt(j, lo, n):
            return work[:, j * TS + lo: j * TS + lo + n]

        def kt(j, lo, n):
            return work[:, 4096 + j * TS + lo: 4096 + j * TS + lo + n]

        def ktok(cl, lo, n):
            return work[:, 8192 + cl * D + lo: 8192 + cl * D + lo + n]

        def vtok(cl, lo, n):
            return work[:, 12288 + cl * 1040 + lo: 12288 + cl * 1040 + lo + n]

        def hsb(m, lo, n):
            return work[:, m * TS + lo: m * TS + lo + n]

        def wqkvo(widx, k):
            return wA[:, widx * 8192 + k * D: widx * 8192 + (k + 1) * D]

        def w1v(k, m):
            return wA[:, k * FF + m * P: k * FF + (m + 1) * P]

        def xres_c(c, lo=0, n=D):
            return xres[:, c * D + lo: c * D + lo + n]

        def attps(h):
            i = h // 2
            if i < 7:
                return PS1[:, (h % 2) * 512 + i * 65: (h % 2) * 512 + i * 65 + 65]
            return PS1[:, 1024 + (h % 2) * 65: 1024 + (h % 2) * 65 + 65]

        def kvps(h):
            j, pr = h // 2, (h % 2) * 64
            buf = PS0 if pr == 0 else PS1
            return buf[pr:pr + 64, 1536 + j * 64: 1536 + j * 64 + 64]

        def ksps(h):
            j, pr = h // 2, (h % 2) * 64
            return PS1[pr:pr + 64, 1400 + j: 1401 + j]

        row_bk = rowb[0:1, 0:1024]
        row_bv = rowb[0:1, 1024:2048]
        row_bo = rowb[0:1, 2048:3072]
        ones_st = rowb[0:1, 3072:3200]

        SV_B1 = 0
        SV_LN = 32
        SV_D16 = 64
        SV_R16 = 80
        SV_BQ = 96
        SV_BK = 104

        def sv(c0, n=1):
            return smallv[:, c0: c0 + n]

        t_iota = S.emit("pool", lambda e: e.iota(
            f32s[:, 0:128], pattern=[[1, 128]], base=0, channel_multiplier=-1,
            allow_small_or_imprecise_dtypes=True), inc="pool")
        S.emit("dve", lambda e: e.tensor_scalar(
            out=identf[:], in0=f32s[:, 0:128], scalar1=0.0, scalar2=None,
            op0=ALU.is_equal), waits=[t_iota], inc="dve")
        S.emit("dve", lambda e: e.tensor_scalar(
            out=identb[:], in0=f32s[:, 0:128], scalar1=0.0, scalar2=None,
            op0=ALU.is_equal), inc="dve")
        S.emit("dve", lambda e: e.tensor_scalar(
            out=maskb[:], in0=f32s[:, 0:128], scalar1=0.0, scalar2=None,
            op0=ALU.is_ge), inc="dve")
        S.emit("dve", lambda e: e.memset(ones_st, 1.0), inc="dve")
        S.emit("dve", lambda e: e.memset(constp[:, 0:1], 1.0 / 256), inc="dve")
        S.emit("dve", lambda e: e.memset(constp[:, 1:2], 0.25), inc="dve")

        stg_free = [None]

        def load_cast(dst_ap, src_ap, ncols, region_free=()):
            assert ncols <= 512
            w = list(region_free) + ([stg_free[0]] if stg_free[0] else [])
            t_d = dma("sp", "sp", stg[:, 0:ncols], src_ap, waits=w)
            t_c = S.emit("act", lambda e: e.activation(
                out=dst_ap, in_=stg[:, 0:ncols], func=AF.Identity),
                inc="act", waits=[t_d])
            stg_free[0] = t_c
            return t_c

        rowf_free = [None]
        f32s_free = [None]

        def load_bias_row(slot_ap, src_ap, extra_waits=()):
            t2 = None
            for hh in range(2):
                t1 = dma("sp", "sp", rowf[0:1, :], src_ap[:, hh * 512:(hh + 1) * 512],
                         waits=list(extra_waits) + ([t2] if t2 else []) +
                               ([rowf_free[0]] if rowf_free[0] else []))
                t2 = S.emit("dve", lambda e, hh=hh: e.tensor_copy(
                    out=slot_ap[:, hh * 512:(hh + 1) * 512], in_=rowf[0:1, :]),
                    waits=[t1], inc="dve")
                rowf_free[0] = t2
            return t2

        ln_prev = {0: None, 1: None}

        def ln_chunk(c, ready=(), skip_norm=False, upto=99):
            xc = xres_c(c)
            par = SV_LN + 16 * (c % 2)
            pw = ([ln_prev[c % 2]] if ln_prev[c % 2] else []) + [t for t in ready if t]
            t_s1 = S.emit("act", lambda e, par=par, xc=xc: e.activation(
                out=sqd[:], in_=xc, func=AF.Identity, accum_out=sv(par, 1)),
                inc="act", waits=pw)
            t_q1 = S.emit("act", lambda e, par=par, xc=xc: e.activation(
                out=sqd[:], in_=xc, func=AF.Square, accum_out=sv(par + 1, 1)),
                inc="act")
            t_m = S.emit("dve", lambda e, par=par: e.tensor_scalar(
                out=sv(par + 3, 1), in0=sv(par, 1), scalar1=1.0 / D, scalar2=None,
                op0=ALU.mult), inc="dve", waits=[t_s1])
            t_a = S.emit("dve", lambda e, par=par: e.tensor_scalar(
                out=sv(par + 5, 1), in0=sv(par + 1, 1), scalar1=1.0 / D, scalar2=LN_EPS,
                op0=ALU.mult, op1=ALU.add), inc="dve", waits=[t_q1])
            t_b = S.emit("dve", lambda e, par=par: e.scalar_tensor_tensor(
                out=sv(par + 6, 1), in0=sv(par + 3, 1), scalar=-1.0,
                in1=sv(par + 3, 1), op0=ALU.mult, op1=ALU.mult), inc="dve",
                waits=[t_q1], force_waits=[t_m])
            t_v = S.emit("dve", lambda e, par=par: e.tensor_tensor(
                out=sv(par + 7, 1), in0=sv(par + 5, 1), in1=sv(par + 6, 1), op=ALU.add),
                inc="dve", force_waits=[t_a, t_b])
            t_sq = S.emit("act", lambda e, par=par: e.activation(
                out=sv(par + 8, 1), in_=sv(par + 7, 1), func=AF.Sqrt),
                inc="act", waits=[t_v])
            t_r = S.emit("dve", lambda e, par=par: e.reciprocal(
                out=sv(par + 9, 1), in_=sv(par + 8, 1)), inc="dve", waits=[t_sq])
            t_nm = S.emit("dve", lambda e, par=par: e.scalar_tensor_tensor(
                out=sv(par + 10, 1), in0=sv(par + 3, 1), scalar=-1.0,
                in1=sv(par + 9, 1), op0=ALU.mult, op1=ALU.mult), inc="dve",
                force_waits=[t_r])
            if skip_norm:
                ln_prev[c % 2] = t_nm
                return t_nm
            t_n = S.emit("act", lambda e, par=par, xc=xc: e.activation(
                out=xc, in_=xc, func=AF.Identity, scale=sv(par + 9, 1),
                bias=sv(par + 10, 1)), inc="act", waits=[t_nm])
            ln_prev[c % 2] = t_n
            return t_n

        repl_free = [None]

        def affine_sweep(g_src, b_src, norm_toks):
            t_g = dma("pool", "io", repl[:], g_src.unsqueeze(0).partition_broadcast(P),
                      waits=[repl_free[0]] if repl_free[0] else [])
            toks = []
            for c in range(NCH):
                toks.append(S.emit("dve", lambda e, c=c: e.tensor_tensor(
                    out=xres_c(c), in0=xres_c(c), in1=repl[:], op=ALU.mult),
                    inc="dve", waits=[t_g, norm_toks[c]]))
            t_b = dma("pool", "io", repl[:], b_src.unsqueeze(0).partition_broadcast(P),
                      waits=toks)
            out = {}
            for c in range(NCH):
                out[c] = S.emit("dve", lambda e, c=c: e.tensor_tensor(
                    out=xres_c(c), in0=xres_c(c), in1=repl[:], op=ALU.add),
                    inc="dve", waits=[t_b])
            repl_free[0] = out[NCH - 1]
            return out

        w2t_free = [None, None]

        def cast_w2_to_scratch(src, dst):
            last = None
            for m in range(FM):
                tc = None
                for hh in range(2):
                    t_d = dma("sp", "sp", stg[:, 0:512],
                              src[m * P:(m + 1) * P, hh * 512:(hh + 1) * 512],
                              waits=[stg_free[0]] if stg_free[0] else [])
                    tc = S.emit("act", lambda e, m=m, hh=hh: e.activation(
                        out=w2t[:, (m % 2) * D + hh * 512:(m % 2) * D + hh * 512 + 512],
                        in_=stg[:, 0:512], func=AF.Identity), inc="act",
                        waits=[t_d] + ([w2t_free[m % 2]] if (hh == 0 and w2t_free[m % 2])
                                       else []))
                    stg_free[0] = tc
                t_s = dma("sp", "sp", dst[m * P:(m + 1) * P, :],
                          w2t[:, (m % 2) * D:(m % 2) * D + D], waits=[tc])
                w2t_free[m % 2] = t_s
                last = t_s
            return last

        def load_qkvo(l, region_free):
            toks = []
            for widx, wd in enumerate((wq_d, wk_d, wv_d, wo_d)):
                last = None
                for k in range(DK):
                    for hh in range(2):
                        last = load_cast(
                            wqkvo(widx, k)[:, hh * 512:(hh + 1) * 512],
                            wd[l, k * P:(k + 1) * P, hh * 512:(hh + 1) * 512], 512,
                            region_free=region_free)
                toks.append(last)
            return toks

        def load_w1(src, region_free):
            toks = []
            for k in range(DK):
                last = None
                for qq in range(8):
                    last = load_cast(
                        wA[:, k * FF + qq * 512: k * FF + (qq + 1) * 512],
                        src[k * P:(k + 1) * P, qq * 512:(qq + 1) * 512], 512,
                        region_free=region_free)
                toks.append(last)
            return toks

        def transpose_to_xt(s, src_chunk_fn, ident, chunk_waits, extra_pe_waits=()):
            evacs = []
            prev_evac = None
            for g in range(8):
                t_t = None
                for i in range(4):
                    idx = g * 4 + i
                    cl, j = idx // DK, idx % DK
                    c = s * 4 + cl
                    slot = 1024 + (j % 4) * 128
                    w = []
                    if i == 0:
                        w = list(extra_pe_waits) + ([prev_evac] if prev_evac else [])
                        if chunk_waits:
                            w += [chunk_waits[s * 4 + k] for k in range(4)]
                    t_t = S.emit("pe", lambda e, c=c, j=j, slot=slot, fn=src_chunk_fn,
                                 ident=ident: e.transpose(
                        PS0[:, slot:slot + 128], fn(c, j), ident[:]),
                        inc="pe" if i == 3 else None, waits=w)
                t_e = None
                for i in range(4):
                    idx = g * 4 + i
                    cl, j = idx // DK, idx % DK
                    slot = 1024 + (j % 4) * 128
                    t_e = S.emit("dve", lambda e, cl=cl, j=j, slot=slot: e.tensor_copy(
                        out=xt[:, j * TS + cl * 128: j * TS + cl * 128 + 128],
                        in_=PS0[:, slot:slot + 128]),
                        inc="dve" if i == 3 else None, waits=[t_t] if i == 0 else [])
                prev_evac = t_e
                evacs.append(t_e)
            return evacs

        def ffn_phase(w2_src, b1_bias_fn, evac_fn, res_fn, chunk_ready, first_waits):
            res_toks = {}
            h_free = list(first_waits)
            f2_ev = []
            for s in range(NSUP):
                S.bar()
                tx = transpose_to_xt(s, lambda c, j: xres_c(c, j * 128, 128), identf,
                                     chunk_ready)
                f1_ev = []
                for m in range(FM):
                    slot = 512 * (m % 2)
                    mm_t = None
                    for k in range(DK):
                        w = []
                        if k == 0:
                            w = [tx[-1]] + ([f1_ev[m - 2]] if m >= 2 else []) + h_free
                        mm_t = S.emit("pe", lambda e, k=k, m=m, slot=slot: e.matmul(
                            PS0[:, slot:slot + TS], w1v(k, m), xt[:, k * TS:(k + 1) * TS],
                            start=(k == 0), stop=(k == DK - 1)),
                            inc="pe" if k == DK - 1 else None, waits=w)
                    f1_ev.append(evac_fn(m, slot, mm_t))
                h_free = []
                f2_mm = None
                for m in range(FM):
                    t_w = dma("pool", "w2", w2t[:, (m % 2) * D:(m % 2) * D + D],
                              w2_src[m * P:(m + 1) * P, :],
                              waits=([w2t_free[m % 2]] if w2t_free[m % 2] else []))
                    for cs in range(4):
                        pso = (PS0 if cs < 2 else PS1)[:, (cs % 2) * D:(cs % 2) * D + D]
                        w = []
                        if cs == 0:
                            w = [t_w, f1_ev[m]]
                            if m == 0:
                                w += f1_ev + f2_ev
                        for hh in range(2):
                            f2_mm = S.emit("pe", lambda e, m=m, cs=cs, pso=pso,
                                           hh=hh: e.matmul(
                                pso[:, hh * 512: hh * 512 + 512], hsb(m, cs * 128, 128),
                                w2t[:, (m % 2) * D + hh * 512:(m % 2) * D + hh * 512 + 512],
                                start=(m == 0), stop=False),
                                inc="pe" if (cs == 3 and hh == 1) else None,
                                waits=w if hh == 0 else [])
                    w2t_free[m % 2] = f2_mm
                f2_ev = []
                for cs in range(4):
                    c = s * 4 + cs
                    pso = (PS0 if cs < 2 else PS1)[:, (cs % 2) * D:(cs % 2) * D + D]
                    bm = None
                    for hh in range(2):
                        bm = S.emit("pe", lambda e, pso=pso, hh=hh: e.matmul(
                            pso[:, hh * 512: hh * 512 + 512], ones_st,
                            row_bo[:, hh * 512:(hh + 1) * 512], start=False, stop=True),
                            inc="pe" if hh == 1 else None, waits=[f2_mm] if hh == 0 else [])
                    t_res = res_fn(c, pso, bm)
                    f2_ev.append(t_res)
                    res_toks[c] = t_res
                h_free = [f2_mm]
            return res_toks, f2_ev

        out_dma_count = 0
        wA_free = []

        for seq in range(n_seqs):
            x_ready = {}
            for c in range(NCH):
                w = []
                if seq > 0 and c == 0:
                    w = S.now("dve", "act") + [("st", 16 * out_dma_count)]
                x_ready[c] = dma("sp", "sp", xres_c(c),
                                 x_in[seq, c * CH:(c + 1) * CH, :], waits=w)
            S.emit("pe", lambda e: None, waits=[x_ready[NCH - 1]])
            S.emit("dve", lambda e: None, waits=[x_ready[NCH - 1]])
            S.emit("act", lambda e: None, waits=[x_ready[NCH - 1]])

            for l in range(n_layers):
                qkvo_ready = load_qkvo(l, wA_free)
                sv_toks = [
                    dma("sp", "sp", sv(SV_B1, 32),
                        b1_d[l: l + 1, :].rearrange("o (m p) -> (o p) m", p=P),
                        waits=S.now("dve", "act"), slow=True),
                    dma("sp", "sp", sv(SV_BQ, 8),
                        bq_d[l: l + 1, :].rearrange("o (m p) -> (o p) m", p=P), slow=True),
                    dma("sp", "sp", sv(SV_BK, 8),
                        bk_d[l: l + 1, :].rearrange("o (m p) -> (o p) m", p=P), slow=True),
                ]
                row_toks = [
                    load_bias_row(row_bk, bk_d[l: l + 1, :], extra_waits=S.now("pe")),
                    load_bias_row(row_bv, bv_d[l: l + 1, :]),
                    load_bias_row(row_bo, bo_d[l: l + 1, :]),
                ]
                if seq == 0:
                    w2_scratch_ready = cast_w2_to_scratch(w2_d[l], w2s_d[l])
                else:
                    w2_scratch_ready = None

                S.bar()
                t_kv0 = S.emit("dve", lambda e: e.memset(kvst[:], 0.0), inc="dve")
                kvbf_ready = S.emit("dve", lambda e: e.memset(kvbf[:], 0.0), inc="dve",
                                    waits=[t_kv0])
                t_vones = S.emit("dve", lambda e: e.memset(
                    vtok(0, 0, 4 * 1040).rearrange("p (c e) -> p c e", e=65)[:, :, 64:65],
                    1.0), inc="dve")

                ln1_norm = {}
                sc_par = {}
                fm_slot_free = {}
                kv_slot_free = []
                att_psum_free = None
                kvst_psum_free = None
                attb_free = None
                ssb_free = {}
                work_free = []

                n_sup_run = 0 if stage == "load" else (1 if (stage or "").startswith("one") else NSUP)
                for s in range(n_sup_run):
                    S.bar()
                    if s == 0:
                        S.emit("pe", lambda e: None,
                               waits=qkvo_ready + sv_toks + row_toks + [t_vones,
                                                                        kvbf_ready])
                    tx = transpose_to_xt(s, lambda c, j: xres_c(c, j * 128, 128),
                                         identf, None)
                    if stage == "tx":
                        work_free = []
                        continue
                    fm_toks = []
                    for widx, dst, bsv in (((0, qt, SV_BQ), (1, kt, SV_BK))
                                           if stage not in ("qkvtok",) else ()):
                        for m in range(DK):
                            slot = 512 * (m % 2) + 1024 * widx
                            ps = PS0[:, slot:slot + TS]
                            mm_t = None
                            for k in range(DK):
                                w = []
                                if k == 0:
                                    w = [tx[-1]] + fm_slot_free.get((widx, m % 2), [])
                                mm_t = S.emit("pe", lambda e, widx=widx, k=k, m=m,
                                              ps=ps: e.matmul(
                                    ps, wqkvo(widx, k)[:, m * 128:(m + 1) * 128],
                                    xt[:, k * TS:(k + 1) * TS],
                                    start=(k == 0), stop=(k == DK - 1)),
                                    inc="pe" if k == DK - 1 else None, waits=w)
                            bap = sv(bsv + m, 1)
                            if stage == "qkvmm":
                                t_o = S.emit("dve", lambda e, dst=dst, m=m, ps=ps: e.tensor_copy(
                                    out=dst(m, 0, TS), in_=ps), inc="dve",
                                    waits=[mm_t] + work_free)
                                fm_slot_free[(widx, m % 2)] = [t_o]
                                fm_toks.append(t_o)
                                continue
                            t_r = S.emit("dve", lambda e, ps=ps, bap=bap: e.tensor_scalar(
                                out=f32s[:, 0:512], in0=ps, scalar1=bap, scalar2=0.0,
                                op0=ALU.add, op1=ALU.max), inc="dve",
                                waits=[mm_t, f32s_free[0]])
                            t_mn = S.emit("dve", lambda e, ps=ps, bap=bap: e.tensor_scalar(
                                out=f32s[:, 512:1024], in0=ps, scalar1=bap, scalar2=0.0,
                                op0=ALU.add, op1=ALU.min), inc="dve", waits=[mm_t])
                            t_ex = S.emit("act", lambda e: e.activation(
                                out=f32s[:, 512:1024], in_=f32s[:, 512:1024],
                                func=AF.Exp), inc="act", waits=[t_mn])
                            t_o = S.emit("dve", lambda e, dst=dst, m=m: e.tensor_tensor(
                                out=dst(m, 0, TS), in0=f32s[:, 0:512],
                                in1=f32s[:, 512:1024], op=ALU.add), inc="dve",
                                waits=[t_ex] + work_free, force_waits=[t_r])
                            f32s_free[0] = t_o
                            fm_slot_free[(widx, m % 2)] = [t_o]
                            fm_toks.append(t_o)
                    kv_toks = []
                    for cl in (range(4) if stage not in ("qkvfm", "qkvmm") else ()):
                        for k in range(DK):
                            w = kv_slot_free if k == 0 else []
                            for hh in range(2):
                                S.emit("pe", lambda e, k=k, cl=cl, hh=hh: e.matmul(
                                    PS1[:, hh * 512: hh * 512 + 512],
                                    xt[:, k * TS + cl * 128: k * TS + cl * 128 + 128],
                                    wqkvo(1, k)[:, hh * 512:(hh + 1) * 512],
                                    start=(k == 0), stop=False),
                                    waits=w if hh == 0 else [])
                                S.emit("pe", lambda e, k=k, cl=cl, hh=hh: e.matmul(
                                    PS1[:, 1024 + hh * 512: 1024 + hh * 512 + 512],
                                    xt[:, k * TS + cl * 128: k * TS + cl * 128 + 128],
                                    wqkvo(2, k)[:, hh * 512:(hh + 1) * 512],
                                    start=(k == 0), stop=False))
                        for hh in range(2):
                            mmk = S.emit("pe", lambda e, hh=hh: e.matmul(
                                PS1[:, hh * 512: hh * 512 + 512], ones_st,
                                row_bk[:, hh * 512:(hh + 1) * 512],
                                start=False, stop=True), inc="pe" if hh == 1 else None)
                            mmv = S.emit("pe", lambda e, hh=hh: e.matmul(
                                PS1[:, 1024 + hh * 512: 1024 + hh * 512 + 512], ones_st,
                                row_bv[:, hh * 512:(hh + 1) * 512],
                                start=False, stop=True), inc="pe" if hh == 1 else None)
                        tk = None
                        for hh in range(2):
                            pss = PS1[:, hh * 512: hh * 512 + 512]
                            t_r = S.emit("dve", lambda e, pss=pss: e.tensor_scalar(
                                out=f32s[:, 0:512], in0=pss, scalar1=0.0, scalar2=None,
                                op0=ALU.max), inc="dve",
                                waits=[mmk, f32s_free[0]])
                            t_mn = S.emit("dve", lambda e, pss=pss: e.tensor_scalar(
                                out=f32s[:, 512:1024], in0=pss, scalar1=0.0, scalar2=None,
                                op0=ALU.min), inc="dve", waits=[mmk])
                            t_ex = S.emit("act", lambda e: e.activation(
                                out=f32s[:, 512:1024], in_=f32s[:, 512:1024],
                                func=AF.Exp), inc="act", waits=[t_mn])
                            tk = S.emit("dve", lambda e, cl=cl, hh=hh: e.tensor_tensor(
                                out=ktok(cl, hh * 512, 512), in0=f32s[:, 0:512],
                                in1=f32s[:, 512:1024], op=ALU.add), inc="dve",
                                waits=[t_ex] + work_free, force_waits=[t_r])
                            f32s_free[0] = tk
                            kv_toks.append(tk)
                        tv = S.emit("dve", lambda e, cl=cl: e.tensor_copy(
                            out=vtok(cl, 0, 1040).rearrange(
                                "p (h e) -> p h e", e=65)[:, :, 0:64],
                            in_=PS1[:, 1024:2048].rearrange("p (h e) -> p h e", e=64)),
                            inc="dve", waits=[mmv] + work_free)
                        kv_toks.append(tv)
                        kv_slot_free = [tk, tv]
                    qkv_done = fm_toks + kv_toks
                    if stage in ("qkv", "qkvmm"):
                        work_free = []
                        continue

                    att_mm = None
                    for cl in ([0] if (stage or "").startswith("one") else range(4)):
                        c = s * 4 + cl
                        att_mm = None
                        kv_toks_par = []
                        ks_toks_par = []
                        for par in (0, 1):
                            heads = list(range(par, H, 2))
                            pr = par * 64
                            S.emit("pe", lambda e: None,
                                   force_waits=[("pe", S.n_inc["pe"])])
                            mm_last = None
                            for ti, h in enumerate(heads):
                                j = h // 2
                                slot = ti * 128 if ti < 4 else 512 + (ti - 4) * 128
                                w = []
                                if ti == 0:
                                    w = (qkv_done if (cl == 0 and par == 0) else []) + \
                                        list(sc_par.get(par, [])) + \
                                        list(sc_par.get(1 - par, [])) + \
                                        ([kvbf_ready] if kvbf_ready else [])
                                mm_last = S.emit("pe", lambda e, j=j, pr=pr, cl=cl,
                                                 slot=slot: e.matmul(
                                    PS0[:, slot:slot + 128],
                                    kt(j, cl * 128, 128)[pr:pr + 64, :],
                                    qt(j, cl * 128, 128)[pr:pr + 64, :],
                                    start=True, stop=True),
                                    inc="pe" if ti == 7 else None, waits=w)
                            evs = []
                            for bk in range(2):
                                a0 = bk * 4
                                t_me = S.emit("dve", lambda e, par=par, a0=a0,
                                              bk=bk: e.tensor_tensor(
                                    out=ssb[:].rearrange(
                                        "p (a two c) -> p a two c", two=2,
                                        c=128)[:, a0:a0 + 4, par, :],
                                    in0=PS0[:, bk * 512: bk * 512 + 512].rearrange(
                                        "p (t c) -> p t c", c=128),
                                    in1=maskb[:].unsqueeze(1).broadcast_to(
                                        [P, 4, 128]),
                                    op=ALU.mult), inc="dve",
                                    waits=[mm_last] + list(ssb_free.get(par, [])))
                                evs.append(t_me)
                            sc_par[par] = evs
                            if stage == "one_ssb" and par == 1:
                                S.emit("dve", lambda e: e.tensor_copy(
                                    out=xres[:, 0:2048], in_=ssb[:]),
                                    inc="dve", waits=evs)
                            if stage in ("sc", "one_ssb"):
                                continue
                            kvt = None
                            for ti, h in enumerate(heads):
                                kvt = S.emit("pe", lambda e, h=h, pr=pr,
                                             cl=cl: e.matmul(
                                    kvps(h),
                                    ktok(cl, h * 64, 64),
                                    vtok(cl, (h * 65), 64),
                                    start=True, stop=True,
                                    tile_position=(0, pr) if pr else None),
                                    inc="pe" if ti == 7 else None,
                                    waits=([kvst_psum_free] if (ti == 0 and par == 0
                                           and kvst_psum_free) else []))
                            kv_toks_par.append(kvt)
                            kst = None
                            for ti, h in enumerate(heads):
                                kst = S.emit("pe", lambda e, h=h, pr=pr,
                                             cl=cl: e.matmul(
                                    ksps(h),
                                    ktok(cl, h * 64, 64),
                                    vtok(cl, 64, 1),
                                    start=True, stop=True,
                                    tile_position=(0, pr) if pr else None),
                                    inc="pe" if ti == 7 else None,
                                    waits=([kvst_psum_free] if (ti == 0 and par == 0
                                           and kvst_psum_free) else []))
                            ks_toks_par.append(kst)
                            if stage == "sckv":
                                continue
                            for ti, h in enumerate(heads):
                                j = h // 2
                                w = evs if ti == 0 else []
                                S.emit("pe", lambda e, h=h, cl=cl: e.matmul(
                                    attps(h),
                                    ssb[:, h * 128:(h + 1) * 128],
                                    vtok(cl, h * 65, 65),
                                    start=True, stop=False),
                                    waits=w + ([att_psum_free] if (ti == 0 and
                                               par == 0 and att_psum_free) else []))
                                att_mm = S.emit("pe", lambda e, h=h, j=j,
                                                pr=pr: e.matmul(
                                    attps(h),
                                    qt(j, cl * 128, 128)[pr:pr + 64, :],
                                    kvbf[pr:pr + 64, j * 65:(j + 1) * 65],
                                    start=False, stop=True),
                                    inc="pe" if ti == 7 else None)
                            t_k1 = S.emit("dve", lambda e, pr=pr, par=par: e.tensor_tensor(
                                out=kvst[pr:pr + 64, 0:520].rearrange(
                                    "p (j e) -> p j e", e=65)[:, :, 0:64],
                                in0=(PS0 if par == 0 else PS1)[pr:pr + 64,
                                    1536:2048].rearrange("p (j e) -> p j e", e=64),
                                in1=kvst[pr:pr + 64, 0:520].rearrange(
                                    "p (j e) -> p j e", e=65)[:, :, 0:64],
                                op=ALU.add), inc="dve", waits=[kvt])
                            kv_toks_par += [t_k1]
                        if stage in ("sc", "sckv", "one_ssb"):
                            ssb_free[0] = [("pe", S.n_inc["pe"])]
                            ssb_free[1] = [("pe", S.n_inc["pe"])]
                            continue
                        for par in (0, 1):
                            pr = par * 64
                            t_k2 = S.emit("dve", lambda e, pr=pr: e.tensor_tensor(
                                out=kvst[pr:pr + 64, 0:520].rearrange(
                                    "p (j e) -> p j e", e=65)[:, :, 64],
                                in0=PS1[pr:pr + 64, 1400:1408],
                                in1=kvst[pr:pr + 64, 0:520].rearrange(
                                    "p (j e) -> p j e", e=65)[:, :, 64],
                                op=ALU.add), inc="dve",
                                waits=[ks_toks_par[par], att_mm])
                            kv_toks_par += [t_k2]
                        kvst_psum_free = kv_toks_par[-1]
                        if stage == "one_num":
                            S.emit("dve", lambda e: e.tensor_copy(
                                out=xres[:, 0:512], in_=PS1[:, 0:512]),
                                inc="dve", waits=[att_mm])
                            S.emit("dve", lambda e: e.tensor_copy(
                                out=xres[:, 512:1024], in_=PS1[:, 512:1024]),
                                inc="dve")
                            S.emit("dve", lambda e: e.tensor_copy(
                                out=xres[:, 1024:1536], in_=PS1[:, 1024:1536]),
                                inc="dve")
                            continue
                        ssb_free[0] = [att_mm]
                        ssb_free[1] = [att_mm]
                        t_d16 = None
                        dgrp = ((0, 0, 0, 7), (1, 512, 0, 7),
                                (0, 1024, 7, 1), (1, 1089, 7, 1))
                        for (par, col, i0_, nh) in dgrp:
                            t_d16 = S.emit("dve", lambda e, par=par, col=col, i0_=i0_,
                                           nh=nh: e.tensor_scalar(
                                out=sv(SV_D16 + par * 8 + i0_, nh),
                                in0=PS1[:, col: col + nh * 65].rearrange(
                                    "p (h e) -> p h e", e=65)[:, :, 64],
                                scalar1=ATT_EPS, scalar2=None, op0=ALU.add),
                                inc="dve", waits=[att_mm])
                        t_r16 = S.emit("dve", lambda e: e.reciprocal(
                            out=sv(SV_R16, 16), in_=sv(SV_D16, 16)), inc="dve",
                            waits=[t_d16])
                        t_att = None
                        t_cp = None
                        for (par, col, i0_, nh) in dgrp:
                            t_cp = S.emit("dve", lambda e, par=par, col=col, nh=nh,
                                   i0_=i0_: e.tensor_copy(
                                out=f32s[:, 0:1024].rearrange(
                                    "p (a two e) -> p a two e", two=2,
                                    e=64)[:, i0_:i0_ + nh, par, :],
                                in_=PS1[:, col: col + nh * 65].rearrange(
                                    "p (h e) -> p h e", e=65)[:, :, 0:64]),
                                inc="dve",
                                waits=[t_r16] + ([attb_free] if attb_free else []))
                        for (par, col, i0_, nh) in dgrp:
                            t_att = S.emit("dve", lambda e, par=par, nh=nh,
                                           i0_=i0_: e.tensor_tensor(
                                out=f32s[:, 0:1024].rearrange(
                                    "p (a two e) -> p a two e", two=2,
                                    e=64)[:, i0_:i0_ + nh, par, :],
                                in0=f32s[:, 0:1024].rearrange(
                                    "p (a two e) -> p a two e", two=2,
                                    e=64)[:, i0_:i0_ + nh, par, :],
                                in1=sv(SV_R16 + par * 8 + i0_, nh).unsqueeze(
                                    2).broadcast_to([P, nh, 64]),
                                op=ALU.mult), inc="dve", force_waits=[t_cp])
                        att_psum_free = t_att
                        if stage == "one_den":
                            S.emit("dve", lambda e: e.tensor_copy(
                                out=xres[:, 0:16], in_=sv(SV_D16, 16)), inc="dve",
                                waits=[t_r16])
                            S.emit("dve", lambda e: e.tensor_copy(
                                out=xres[:, 16:32], in_=sv(SV_R16, 16)), inc="dve")
                            S.emit("dve", lambda e, cl=cl: e.tensor_copy(
                                out=xres[:, 1024:2048], in_=f32s[:, 0:1024]),
                                inc="dve", waits=[t_att])
                            continue
                        kvbf_ready = S.emit("dve", lambda e: e.tensor_copy(
                            out=kvbf[:], in_=kvst[:]), inc="dve",
                            waits=[att_mm], force_waits=[kv_toks_par[-1]])
                        if stage == "attn":
                            att_psum_free = t_att
                            continue
                        at_last_t = None
                        at_ev = []
                        for g in range(2):
                            t_t = None
                            for i in range(4):
                                j = g * 4 + i
                                slot = 1024 + (j % 4) * 128
                                w = []
                                if i == 0:
                                    w = [t_att] + ([at_ev[-1]] if at_ev else [])
                                t_t = S.emit("pe", lambda e, j=j, slot=slot: e.transpose(
                                    PS0[:, slot:slot + 128],
                                    f32s[:, j * 128:(j + 1) * 128],
                                    identf[:]), inc="pe" if i == 3 else None, waits=w)
                            at_last_t = t_t
                            t_e = None
                            for i in range(4):
                                j = g * 4 + i
                                slot = 1024 + (j % 4) * 128
                                t_e = S.emit("dve", lambda e, j=j, cl=cl,
                                             slot=slot: e.tensor_copy(
                                    out=xt[:, j * TS + cl * 128: j * TS + cl * 128 + 128],
                                    in_=PS0[:, slot:slot + 128]),
                                    inc="dve" if i == 3 else None,
                                    waits=[t_t] if i == 0 else [])
                            at_ev.append(t_e)
                        attb_free = at_last_t
                        if stage in ("dbgatt", "one_att"):
                            S.emit("dve", lambda e, c=c: e.tensor_copy(
                                out=xres_c(c), in_=f32s[:, 0:1024]),
                                inc="dve", waits=[t_att])
                            att_psum_free = t_att
                            continue
                        for k in range(DK):
                            w = (at_ev + [att_psum_free]) if k == 0 else []
                            for hh in range(2):
                                S.emit("pe", lambda e, k=k, cl=cl, hh=hh: e.matmul(
                                    PS1[:, hh * 512: hh * 512 + 512],
                                    xt[:, k * TS + cl * 128: k * TS + cl * 128 + 128],
                                    wqkvo(3, k)[:, hh * 512:(hh + 1) * 512],
                                    start=(k == 0), stop=False),
                                    waits=w if hh == 0 else [])
                        wo_mm = None
                        for hh in range(2):
                            wo_mm = S.emit("pe", lambda e, hh=hh: e.matmul(
                                PS1[:, hh * 512: hh * 512 + 512], ones_st,
                                row_bo[:, hh * 512:(hh + 1) * 512],
                                start=False, stop=True), inc="pe" if hh == 1 else None)
                        t_res = S.emit("dve", lambda e, c=c: e.tensor_tensor(
                            out=xres_c(c), in0=PS1[:, 0:1024], in1=xres_c(c), op=ALU.add),
                            inc="dve", waits=[wo_mm])
                        att_psum_free = t_res
                        if stage != "wo":
                            up = {"ln1a": 98, "lnu1": 1, "lnu2": 2,
                                  "lnu3": 3}.get(stage, 99)
                            ln1_norm[c] = ln_chunk(c, ready=[t_res],
                                                   skip_norm=(stage == "ln1a"),
                                                   upto=up)
                        if stage in ("ln1", "ln1a", "lnu1", "lnu2", "lnu3"):
                            continue
                    work_free = [att_mm] if att_mm else [("pe", S.n_inc["pe"])]

                if stage is not None:
                    if stage in ("ln1", "ln1a", "lnu1", "lnu2", "lnu3"):
                        dma("pool", "st", out_d[seq, 1920:2048, 0:192], smallv[:],
                            waits=S.now("dve", "act"))
                        out_dma_count += 1
                    wA_free = S.now("pe")
                    continue
                ln1_out = affine_sweep(g1_d[l, :], be1_d[l, :], ln1_norm)

                S.bar()
                w1_ready = load_w1(w1_d[l], S.now("pe"))
                b2_tok = load_bias_row(row_bo, b2_d[l: l + 1, :], extra_waits=S.now("pe"))
                S.emit("pe", lambda e: None,
                       waits=w1_ready + [b2_tok] +
                             ([w2_scratch_ready] if w2_scratch_ready else []))

                def gelu_evac(m, slot, mm_t):
                    return S.emit("act", lambda e, m=m, slot=slot: e.activation(
                        out=hsb(m, 0, TS), in_=PS0[:, slot:slot + TS], func=AF.Gelu,
                        bias=sv(SV_B1 + m, 1)), inc="act", waits=[mm_t])

                def ffn_res(c, pso, bm):
                    t = S.emit("dve", lambda e, c=c, pso=pso: e.tensor_tensor(
                        out=xres_c(c), in0=pso, in1=xres_c(c), op=ALU.add),
                        inc="dve", waits=[bm])
                    return t

                res_toks, f2_ev = ffn_phase(w2s_d[l], None, gelu_evac, ffn_res,
                                            ln1_out, [])
                ln2_norm = {}
                for c in range(NCH):
                    ln2_norm[c] = ln_chunk(c, ready=[res_toks[c]])
                ln2_out = affine_sweep(g2_d[l, :], be2_d[l, :], ln2_norm)
                wA_free = S.now("pe")

            if do_tail:
                S.bar()
                fin_norm = {}
                for c in range(NCH):
                    fin_norm[c] = ln_chunk(c, ready=[ln2_out[c]])
                fin_out = affine_sweep(gf_d, bef_d, fin_norm)
                top1 = {}
                for c in range(NCH):
                    t_x = dma("pool", "io", repl[:], x_in[seq, c * CH:(c + 1) * CH, :],
                              waits=[repl_free[0]] if repl_free[0] else [])
                    t_a = S.emit("dve", lambda e, c=c: e.tensor_tensor(
                        out=xres_c(c), in0=repl[:], in1=xres_c(c), op=ALU.add),
                        inc="dve", waits=[t_x, fin_out[c]])
                    repl_free[0] = t_a
                    top1[c] = ln_chunk(c, ready=[t_a])
                S.bar()
                w1_ready = load_w1(fw1_d, S.now("pe"))
                t_fb1 = dma("sp", "sp", sv(SV_B1, 32),
                            fb1_d.unsqueeze(0).rearrange("o (m p) -> (o p) m", p=P),
                            waits=S.now("dve", "act"), slow=True)
                t_fb1b = S.emit("dve", lambda e: e.tensor_scalar(
                    out=sv(SV_B1, 32), in0=sv(SV_B1, 32), scalar1=256.0, scalar2=0.5,
                    op0=ALU.mult, op1=ALU.add), inc="dve", waits=[t_fb1])
                if seq == 0:
                    fw2_ready = cast_w2_to_scratch(fw2_d, fw2s_d)
                else:
                    fw2_ready = None
                t_R = dma("pool", "io", repl[:], fb2_d.unsqueeze(0).partition_broadcast(P),
                          waits=[repl_free[0]] if repl_free[0] else [])
                t_R2 = S.emit("dve", lambda e: e.tensor_scalar(
                    out=repl[:], in0=repl[:], scalar1=256.0, scalar2=0.5,
                    op0=ALU.mult, op1=ALU.add), inc="dve", waits=[t_R])
                repl_free[0] = None
                S.emit("pe", lambda e: None,
                       waits=w1_ready + [t_fb1b, t_R2] +
                             ([fw2_ready] if fw2_ready else []))

                act_gelu_prev = [None, None]

                def pwl_evac(m, slot, mm_t):
                    ps = PS0[:, slot:slot + TS]
                    w0 = [mm_t] + [t for t in act_gelu_prev if t]
                    t1 = S.emit("dve", lambda e, ps=ps, m=m: e.tensor_scalar(
                        out=f32s[:, 0:512], in0=ps, scalar1=256.0,
                        scalar2=sv(SV_B1 + m, 1), op0=ALU.mult, op1=ALU.add),
                        inc="dve", waits=w0)
                    t2 = S.emit("dve", lambda e: e.tensor_copy(
                        out=i32a[:], in_=f32s[:, 0:512]), inc="dve", force_waits=[t1])
                    t3 = S.emit("dve", lambda e: e.tensor_tensor(
                        out=u8m[:], in0=i32a[:], in1=f32s[:, 0:512], op=ALU.is_gt),
                        inc="dve", force_waits=[t2])
                    t4 = S.emit("dve", lambda e: e.tensor_tensor(
                        out=i32a[:], in0=i32a[:], in1=u8m[:], op=ALU.subtract),
                        inc="dve", force_waits=[t3])
                    t5 = S.emit("dve", lambda e: e.tensor_scalar(
                        out=i32b[:], in0=i32a[:], scalar1=-64, scalar2=None,
                        op0=ALU.bitwise_and), inc="dve", force_waits=[t4])
                    t6 = S.emit("act", lambda e: e.activation(
                        out=f32s[:, 0:512], in_=i32b[:], func=AF.Gelu,
                        scale=constp[:, 0:1], bias=0.0), inc="act", waits=[t5])
                    t7 = S.emit("act", lambda e: e.activation(
                        out=f32s[:, 512:1024], in_=i32b[:], func=AF.Gelu,
                        scale=constp[:, 0:1], bias=constp[:, 1:2]), inc="act", waits=[t5])
                    act_gelu_prev[0], act_gelu_prev[1] = t6, t7
                    t8 = S.emit("dve", lambda e: e.tensor_tensor(
                        out=f32s[:, 512:1024], in0=f32s[:, 512:1024],
                        in1=f32s[:, 0:512], op=ALU.subtract), inc="dve", waits=[t6, t7])
                    t9 = S.emit("dve", lambda e: e.tensor_scalar(
                        out=i32b[:], in0=i32a[:], scalar1=63, scalar2=None,
                        op0=ALU.bitwise_and), inc="dve", waits=[t6, t7],
                        force_waits=[t8])
                    t10 = S.emit("dve", lambda e: e.scalar_tensor_tensor(
                        out=f32s[:, 512:1024], in0=i32b[:], scalar=1.0 / 64,
                        in1=f32s[:, 512:1024], op0=ALU.mult, op1=ALU.mult),
                        inc="dve", force_waits=[t9])
                    t11 = S.emit("dve", lambda e: e.tensor_tensor(
                        out=f32s[:, 512:1024], in0=f32s[:, 512:1024],
                        in1=f32s[:, 0:512], op=ALU.add), inc="dve", force_waits=[t10])
                    t12 = S.emit("dve", lambda e: e.tensor_scalar(
                        out=u8m[:], in0=i32a[:], scalar1=1024, scalar2=None,
                        op0=ALU.is_gt), inc="dve", force_waits=[t11])
                    t13 = S.emit("dve", lambda e: e.tensor_scalar(
                        out=f32s[:, 0:512], in0=i32a[:], scalar1=1.0 / 256, scalar2=None,
                        op0=ALU.mult), inc="dve", force_waits=[t12])
                    t14 = S.emit("dve", lambda e: e.copy_predicated(
                        out=f32s[:, 512:1024], mask=u8m[:], data=f32s[:, 0:512]),
                        inc="dve", force_waits=[t13])
                    t15 = S.emit("dve", lambda e: e.tensor_scalar(
                        out=f32s[:, 0:512], in0=f32s[:, 512:1024], scalar1=256.0,
                        scalar2=0.5, op0=ALU.mult, op1=ALU.add), inc="dve",
                        force_waits=[t14])
                    t16 = S.emit("dve", lambda e: e.tensor_copy(
                        out=i32b[:], in_=f32s[:, 0:512]), inc="dve", force_waits=[t15])
                    t17 = S.emit("dve", lambda e: e.tensor_tensor(
                        out=u8m[:], in0=i32b[:], in1=f32s[:, 0:512], op=ALU.is_gt),
                        inc="dve", force_waits=[t16])
                    t18 = S.emit("dve", lambda e, m=m: e.tensor_tensor(
                        out=hsb(m, 0, TS), in0=i32b[:], in1=u8m[:], op=ALU.subtract),
                        inc="dve", force_waits=[t17])
                    return t18

                def fblk_res(c, pso, bm):
                    toks = []
                    for hh in range(2):
                        t1 = S.emit("dve", lambda e, pso=pso, hh=hh: e.tensor_tensor(
                            out=f32s[:, hh * 512: hh * 512 + 512],
                            in0=pso[:, hh * 512: hh * 512 + 512],
                            in1=repl[:, hh * 512: hh * 512 + 512], op=ALU.add),
                            inc="dve", waits=[bm])
                        t2 = S.emit("dve", lambda e, hh=hh: e.tensor_copy(
                            out=i32a[:], in_=f32s[:, hh * 512: hh * 512 + 512]),
                            inc="dve", force_waits=[t1])
                        t3 = S.emit("dve", lambda e, hh=hh: e.tensor_tensor(
                            out=u8m[:], in0=i32a[:],
                            in1=f32s[:, hh * 512: hh * 512 + 512], op=ALU.is_gt),
                            inc="dve", force_waits=[t2])
                        t4 = S.emit("dve", lambda e: e.tensor_tensor(
                            out=i32a[:], in0=i32a[:], in1=u8m[:], op=ALU.subtract),
                            inc="dve", force_waits=[t3])
                        t5 = S.emit("dve", lambda e, c=c, hh=hh: e.scalar_tensor_tensor(
                            out=xres_c(c, hh * 512, 512), in0=i32a[:], scalar=1.0 / 256,
                            in1=xres_c(c, hh * 512, 512), op0=ALU.mult, op1=ALU.add),
                            inc="dve", force_waits=[t4])
                        toks.append(t5)
                    return toks[-1]

                t_zrow = S.emit("dve", lambda e: e.memset(row_bo, 0.0), inc="dve",
                                waits=S.now("pe"))
                S.emit("pe", lambda e: None, waits=[t_zrow])

                res_toks, f2_ev = ffn_phase(fw2s_d, None, pwl_evac, fblk_res, top1, [])
                for c in range(NCH):
                    t_ln = ln_chunk(c, ready=[res_toks[c]])
                    dma("pool", "st", out_d[seq, c * CH:(c + 1) * CH, :], xres_c(c),
                        waits=[t_ln])
                    out_dma_count += 1
                wA_free = S.now("pe")
                repl_free[0] = S.now("dve")[0]
            else:
                S.bar()
                for c in range(NCH):
                    dma("pool", "st", out_d[seq, c * CH:(c + 1) * CH, :], xres_c(c),
                        waits=S.now("dve", "act", "pe") + [x_ready[c]])
                    out_dma_count += 1

        S.emit("pool", lambda e: None, waits=[("st", 16 * out_dma_count)])

        S.finalize()
        eng_map = {"pe": blk.tensor, "dve": blk.vector, "act": blk.scalar,
                   "pool": blk.gpsimd, "sp": blk.sync}
        for name in ENGINES:
            ops = S.q[name]
            deco = eng_map[name]

            @deco
            def _(e, ops=ops):
                for fn in ops:
                    fn(e)

    return nc


def kernel(**inputs):
    x = np.ascontiguousarray(inputs["L0_attn_x"])
    nc = build_program()
    base = {
        "wq": inputs["L0_attn_Wq"], "wk": inputs["L0_attn_Wk"],
        "wv": inputs["L0_attn_Wv"], "wo": inputs["L0_attn_Wo"],
        "w1": inputs["L0_attn_W1"], "w2": inputs["L0_attn_W2"],
        "bq": inputs["L0_attn_bq"], "bk": inputs["L0_attn_bk"],
        "bv": inputs["L0_attn_bv"], "bo": inputs["L0_attn_bo"],
        "b1": inputs["L0_attn_b1"], "b2": inputs["L0_attn_b2"],
        "g1": inputs["L0_attn_g1"], "be1": inputs["L0_attn_be1"],
        "g2": inputs["L0_attn_g2"], "be2": inputs["L0_attn_be2"],
        "gf": inputs["L0_attn_gf"], "bef": inputs["L0_attn_bef"],
        "fw1": inputs["L0_ffn_w1"], "fb1": inputs["L0_ffn_b1"],
        "fw2": inputs["L0_ffn_w2"], "fb2": inputs["L0_ffn_b2"],
    }
    base = {k: np.ascontiguousarray(v) for k, v in base.items()}
    in_maps = []
    for core in range(8):
        m = dict(base)
        m["x"] = np.ascontiguousarray(x[2 * core: 2 * core + 2])
        in_maps.append(m)
    res = run_bass_kernel_spmd(nc, in_maps, core_ids=list(range(8)))
    return np.concatenate([r["out"] for r in res.results], axis=0)


if __name__ == "__main__":
    print("building program ...")
    nc = build_program()
    n = sum(1 for _ in nc.all_instructions())
    print("build ok,", n, "instructions")
